# version 30 (final)
# baseline (speedup 1.0000x reference)
import numpy as np
import ml_dtypes
from contextlib import ExitStack

import concourse.bass as bass
import concourse.mybir as mybir
from concourse.bass_utils import run_bass_kernel_spmd

F32 = mybir.dt.float32
BF16 = mybir.dt.bfloat16
AF = mybir.ActivationFunctionType
ALU = mybir.AluOpType

D = 1024
NCH = 8
ES = 64
FW = 64

ENGINES = ("sync", "scalar", "vector", "tensor", "gpsimd")


def build_nc(S=4096, debug_scratch=False, stop_phase=99):
    nc = bass.Bass("TRN2", target_bir_lowering=False)
    G = S // 16
    NT = S // 128
    NQ = S // 256
    NB = S // 512
    NKT = S // 128
    QC = S // 4
    QG = QC // 16
    SELC = 16 * G

    SCR = "ExternalOutput" if debug_scratch else "Internal"

    x_d = nc.dram_tensor("x", [S, D], F32, kind="ExternalInput").ap()
    wqkv_d = nc.dram_tensor("wqkv", [D, 3 * D], F32, kind="ExternalInput").ap()
    bqkv_pc_d = nc.dram_tensor("bqkv_pc", [128, 16], F32, kind="ExternalInput").ap()
    bv_t_d = nc.dram_tensor("bv_t", [128, D], F32, kind="ExternalInput").ap()
    wout_d = nc.dram_tensor("wout", [D, D], F32, kind="ExternalInput").ap()
    bout_t_d = nc.dram_tensor("bout_t", [128, D], F32, kind="ExternalInput").ap()
    ctrl_d = nc.dram_tensor("ctrl", [D, ES], F32, kind="ExternalInput").ap()
    f1_d = nc.dram_tensor("f1", [D, ES * FW], F32, kind="ExternalInput").ap()
    f2_d = nc.dram_tensor("f2", [ES, FW * D], F32, kind="ExternalInput").ap()
    fbias_t_d = nc.dram_tensor("fbias_t", [FW, ES], F32, kind="ExternalInput").ap()
    tie_d = nc.dram_tensor("tie", [ES, S], BF16, kind="ExternalInput").ap()
    glmask_d = nc.dram_tensor("glmask", [128, 8], F32, kind="ExternalInput").ap()
    ident_d = nc.dram_tensor("ident", [128, 128], BF16, kind="ExternalInput").ap()

    y_d = nc.dram_tensor("y", [S, D], F32, kind="ExternalOutput").ap()

    xbf_d = nc.dram_tensor("xbf", [S, D], BF16, kind=SCR).ap()
    qsp_d = nc.dram_tensor("qsp", [NCH, 128, S], BF16, kind=SCR).ap()
    hsp_d = nc.dram_tensor("hsp", [S, D], BF16, kind=SCR).ap()
    intersp_d = nc.dram_tensor("intersp", [ES, G, D], BF16, kind=SCR).ap()
    f1b_d = nc.dram_tensor("f1b", [D, ES * FW], BF16, kind=SCR).ap()
    f2b_d = nc.dram_tensor("f2b", [ES, FW * D], BF16, kind=SCR).ap()

    ISG_END = 7168 + 4 * 1024 + 5 * 4096 + 2 * 512
    es_ = ExitStack()
    sb = lambda name, shape, dt: es_.enter_context(nc.sbuf_tensor(name, shape, dt))

    A = sb("arenaA", [128, 32768], BF16)
    B = sb("arenaB", [128, max(S * NCH, ISG_END)], BF16)
    C_ = sb("arenaC", [128, max(NT * 1024,
                                NT * 512 + S + 512 + NT * 64 + QC + 4096 + QC + 64)], BF16)
    FP = sb("arenaF", [128, 2048], F32)
    QS = sb("qstage", [128, 1024], BF16)
    BQ = sb("s_bqkv_pc", [128, 16], F32)
    FBT = sb("s_fbias_t", [64, 64], F32)
    SM = sb("smalls", [128, 4], F32)
    GLM = sb("s_glmask", [128, 8], F32)
    GMX = sb("s_gmax", [64, 64], F32)
    BN = sb("bnbuf", [128, 16], F32)
    ONE_T = sb("ones_t", [128, 128], BF16)
    IDT = sb("s_ident", [128, 128], BF16)
    BOUTB = sb("s_boutb", [1, 1024], BF16)

    pb = [es_.enter_context(nc.psum_tensor(f"pb{i}", [128, 512], F32))
          for i in range(8)]

    XB0 = 24576
    WO, QB, EB, CX = 0, 8192, 12288, 13312
    HFA = 17408
    PB = 0
    TIE = NT * 512
    CBF = TIE + S
    PTM = CBF + 512
    PQ = PTM + NT * 64
    PQ2 = PQ + QC + 4 * 1024
    F1B, INB, F2B, ISG = 0, 2048, 3072, 7168
    ITO = ISG + 4 * 1024
    PTO = ITO + 5 * 4096

    CHANNELS = ["xq0", "xq1", "xq2", "xq3", "sm", "wo", "bout", "cb", "tie", "hfm",
                "wc0", "wc1", "wc2", "wc3", "wc4", "wc5", "wc6", "wc7",
                "wck0", "wck1", "wck2", "wck3", "wck4", "wck5", "wck6", "wck7",
                "wcv0", "wcv1", "wcv2", "wcv3", "wcv4", "wcv5", "wcv6", "wcv7",
                "xa0", "xa1", "q0", "q1", "xr0", "xr1", "xr2", "xr3",
                "qs0", "qs1", "h0", "h1", "is0", "is1", "is2", "is3",
                "f10", "f11", "f12", "f13", "f20", "f21", "f22", "f23",
                "hf0", "hf1", "hf2", "hf3",
                "hg0", "hg1", "hg2", "hg3", "ig0", "ig1", "ig2",
                "ha0", "ha1", "ha2", "ha3", "y0", "y1",
                "fc1", "fc2", "ib0", "ib1", "ib2", "ib3", "ib4",
                "ig3", "ig4", "hh0", "hh1", "hh2", "hh3", "hs1", "hs3"]
    P_cnt = {s: 0 for s in CHANNELS + ["sT", "sV", "sC"]}
    P_ops = {e: [] for e in ENGINES}
    sems = {}

    def op(eng, fn):
        P_ops[eng].append(fn)

    def dma(eng, chan, dst, src, **kw):
        P_cnt[chan] += 16
        op(eng, lambda e, d=dst, s=src, kw=kw, sm=chan:
           e.dma_start(d, s, **kw).then_inc(sems[sm], 16))
        return P_cnt[chan]

    waited = {}

    def wait(eng, sname, v):
        if v and v > waited.get((eng, sname), 0):
            waited[(eng, sname)] = v
            op(eng, lambda e, sm=sname, vv=v: e.wait_ge(sems[sm], vv))

    def t_mm(out, lhsT, rhs, start, stop, inc=False):
        if inc:
            P_cnt["sT"] += 1
            op("tensor", lambda e, o=out, l=lhsT, r=rhs, st=start, sp=stop:
               e.matmul(o, l, r, start=st, stop=sp).then_inc(sems["sT"], 1))
        else:
            op("tensor", lambda e, o=out, l=lhsT, r=rhs, st=start, sp=stop:
               e.matmul(o, l, r, start=st, stop=sp))
        return P_cnt["sT"]

    def eng_op(eng, sname, method, *args, inc=False, **kw):
        if inc:
            P_cnt[sname] += 1
            op(eng, lambda e, a=args, k=kw, m=method:
               getattr(e, m)(*a, **k).then_inc(sems[sname], 1))
        else:
            op(eng, lambda e, a=args, k=kw, m=method: getattr(e, m)(*a, **k))
        return P_cnt[sname]

    def v_op(method, *args, inc=False, **kw):
        return eng_op("vector", "sV", method, *args, inc=inc, **kw)

    def c_op(method, *args, inc=False, **kw):
        return eng_op("scalar", "sC", method, *args, inc=inc, **kw)

    def finalize():
        for s in list(P_cnt.keys()):
            sems[s] = es_.enter_context(nc.semaphore(s))
        block = es_.enter_context(nc.Block())
        with es_:
            for eng_name in ENGINES:
                ops = P_ops[eng_name]
                getattr(block, eng_name)(lambda e, ops=ops: [f(e) for f in ops])
        return nc

    def finish_early():
        for s in list(P_cnt.keys()):
            wait("gpsimd", s, P_cnt[s])
        return finalize()

    XCN = min(4, NB) if NB >= 1 else 1
    XCR = S // XCN
    dma("gpsimd", "xq0", xbf_d[0:XCR, :], x_d[0:XCR, :])
    for koff, pref in ((0, "wc"), (1024, "wck"), (2048, "wcv")):
        for k in range(NCH):
            dma("gpsimd", "%s%d" % (pref, k),
                A[:, k * 3072 + koff: k * 3072 + koff + 1024],
                wqkv_d[k * 128:(k + 1) * 128, koff:koff + 1024])
    dma("sync", "sm", BQ[:], bqkv_pc_d)
    dma("sync", "sm", GLM[:], glmask_d)
    dma("sync", "sm", IDT[:], ident_d)
    dma("sync", "sm", FBT[:], fbias_t_d)
    sm_done = dma("sync", "sm", FP[:, 0:1024], bv_t_d)
    v_op("memset", SM[:, 0:1], -4.0)
    v_op("memset", SM[:, 1:2], 1e-5)
    v_op("memset", ONE_T[:], 1.0)
    init_v = v_op("drain", inc=True)

    bank_free = [("sV", 0)] * 8
    xblk_T = [0, 0]

    xq_emitted = 1
    for b in range(NB):
        buf = b % 2
        xoff = XB0 + buf * 4096
        xchan = "xa%d" % buf
        while xq_emitted < XCN and xq_emitted * XCR <= (b + 2) * 512:
            i = xq_emitted
            wait("gpsimd", "sT", P_cnt["sT"])
            dma("gpsimd", "xq%d" % i, xbf_d[i * XCR:(i + 1) * XCR, :],
                x_d[i * XCR:(i + 1) * XCR, :])
            xq_emitted += 1
        wait("sync", "xq%d" % min((b * 512) // XCR, XCN - 1), 16)
        wait("sync", "sT", xblk_T[buf])
        xv = 0
        for c in range(NCH):
            xv = dma("sync", xchan, A[:, xoff + c * 512: xoff + (c + 1) * 512],
                     xbf_d[b * 512:(b + 1) * 512, c * 128:(c + 1) * 128],
                     transpose=True)

        wait("tensor", xchan, xv)

        for kind in ("Q", "K"):
            koff = 0 if kind == "Q" else 1024
            for m in range(NCH):
                pbi = m % 4
                sem, val = bank_free[pbi]
                wait("tensor", sem, val)
                tv = 0
                for k in range(NCH):
                    if b == 0 and m == 0:
                        wait("tensor",
                             ("wc%d" if kind == "Q" else "wck%d") % k, 16)
                    lhsT = A[:, k * 3072 + koff + m * 128:
                             k * 3072 + koff + m * 128 + 128]
                    rhs = A[:, xoff + k * 512: xoff + (k + 1) * 512]
                    tv = t_mm(pb[pbi][:], lhsT, rhs, k == 0, k == NCH - 1,
                              inc=(k == NCH - 1))
                wait("scalar", "sT", tv)
                wait("scalar", "sm", sm_done)
                if kind == "Q":
                    slot = m % 2
                    qchan = "qs%d" % slot
                    wait("scalar", qchan, P_cnt[qchan])
                    qsl = QS[:, slot * 512: slot * 512 + 512]
                    cv = c_op("activation", qsl, pb[pbi][:], AF.Identity,
                              inc=True, bias=BQ[:, m:m + 1])
                    wait("scalar", "sC", cv)
                    dma("scalar", qchan, qsp_d[m, :, b * 512:(b + 1) * 512], qsl)
                else:
                    cv = c_op("activation",
                              B[:, m * S + b * 512: m * S + (b + 1) * 512],
                              pb[pbi][:], AF.Identity, inc=True,
                              bias=BQ[:, 8 + m:8 + m + 1])
                bank_free[pbi] = ("sC", cv)

        wait("vector", "sV", init_v)
        wait("vector", "sm", sm_done)
        for sub in range(4):
            tt = b * 4 + sub
            for half in range(2):
                pbi = 4 + (sub * 2 + half) % 4
                sem, val = bank_free[pbi]
                wait("tensor", sem, val)
                tv = 0
                for k in range(NCH):
                    if b == 0 and sub == 0 and half == 0:
                        wait("tensor", "wcv%d" % k, 16)
                    lhsT = A[:, xoff + k * 512 + sub * 128:
                             xoff + k * 512 + sub * 128 + 128]
                    rhs = A[:, k * 3072 + 2048 + half * 512:
                            k * 3072 + 2048 + (half + 1) * 512]
                    tv = t_mm(pb[pbi][:], lhsT, rhs, k == 0, k == NCH - 1,
                              inc=(k == NCH - 1))
                wait("vector", "sT", tv)
                vv = v_op("tensor_tensor",
                          C_[:, tt * 1024 + half * 512: tt * 1024 + (half + 1) * 512],
                          pb[pbi][:], FP[:, half * 512:(half + 1) * 512],
                          ALU.add, inc=True)
                bank_free[pbi] = ("sV", vv)
        xblk_T[buf] = P_cnt["sT"]

    qkv_T_done = P_cnt["sT"]
    if stop_phase <= 1:
        return finish_early()

    wait("gpsimd", "sT", qkv_T_done)
    for c in range(NCH):
        dma("gpsimd", "wo", A[:, WO + c * 1024: WO + (c + 1) * 1024],
            wout_d[c * 128:(c + 1) * 128, :])
    wout_done = P_cnt["wo"]
    bout_done = dma("sync", "bout", FP[:, 1024:2048], bout_t_d)
    fc1_done = 0
    for i in range(4):
        fc1_done = dma("gpsimd", "fc1", f1b_d[i * 256:(i + 1) * 256, :],
                       f1_d[i * 256:(i + 1) * 256, :])
    fc2_done = 0
    for i in range(4):
        fc2_done = dma("gpsimd", "fc2", f2b_d[i * 16:(i + 1) * 16, :],
                       f2_d[i * 16:(i + 1) * 16, :])

    qload = [0] * (NQ + 2)

    def load_q(qt):
        bufq = qt % 2
        dst = A[:, QB + bufq * 2048: QB + (bufq + 1) * 2048].rearrange(
            "p (c q) -> p c q", c=NCH)
        src = qsp_d[:, :, qt * 256:(qt + 1) * 256].rearrange("c p q -> p c q")
        qload[qt] = dma("sync", "q%d" % bufq, dst, src)

    wait("sync", "qs0", P_cnt["qs0"])
    wait("sync", "qs1", P_cnt["qs1"])
    wait("sync", "sT", qkv_T_done)
    load_q(0)
    if NQ > 1:
        load_q(1)

    xres_free = [0, 0]
    h_spill = [0, 0]
    wait("vector", "bout", bout_done)
    boutb_v = v_op("tensor_copy", BOUTB[:], FP[0:1, 1024:2048], inc=True)
    v_op("drain")

    wait("tensor", "sC", P_cnt["sC"])
    wait("tensor", "sV", P_cnt["sV"])
    carry_sc = (0, 0)
    carry_exp = ()
    ha_next = 0
    prev_kt_T = [0] * NKT
    for qt in range(NQ):
        qbuf = qt % 2
        qcols = QB + qbuf * 2048
        if qt == 0:
            wait("tensor", "q%d" % qbuf, qload[qt])
        for pbi in (0, 1, 7):
            sem, val = bank_free[pbi]
            wait("tensor", sem, val)

        xrl_t = [0, 0]
        for sub in range(2):
            tt = qt * 2 + sub
            slot = tt % 2
            wait("sync", "sT", xres_free[slot])
            wait("sync", "h%d" % slot, h_spill[slot])
            xrl_t[slot] = dma("sync", "xr%d" % slot,
                              A[:, 15360 + slot * 1024: 15360 + (slot + 1) * 1024],
                              xbf_d[tt * 128:(tt + 1) * 128, :])

        kt_T = [0] * NKT
        exp_v = [0] * NKT

        def scores_mms(kt, qc_t=None):
            spp = kt % 2
            qc_t = qcols if qc_t is None else qc_t
            tv = 0
            for k in range(NCH):
                lhsT = B[:, k * S + kt * 128: k * S + kt * 128 + 128]
                rhs = A[:, qc_t + k * 256: qc_t + (k + 1) * 256]
                tv = t_mm(pb[spp][:, 0:256], lhsT, rhs, k == 0, k == NCH - 1,
                          inc=(k == NCH - 1))
            return tv

        sc_T = [0] * (NKT + 1)
        if qt == 0:
            sc_T[0] = scores_mms(0)
        else:
            for kk, vv in enumerate(carry_sc):
                sc_T[kk] = vv
        ctx_bank_waited = False
        NHE = min(4, NKT)
        NHS = min(6, NKT)
        for kt in range(NKT):
            sp = kt % 2
            esl4 = kt % 4
            ebuf = A[:, EB + esl4 * 256: EB + (esl4 + 1) * 256]
            if qt > 0 and kt < NHE:
                exp_v[kt] = carry_exp[kt]
            else:
                wait("scalar", "sT", sc_T[kt])
                if kt >= 4:
                    wait("scalar", "sT", kt_T[kt - 4])
                exp_v[kt] = c_op("activation", ebuf, pb[sp][:, 0:256], AF.Exp,
                                 inc=True, scale=1.0 / 32.0, bias=SM[:, 0:1])
            if kt + 1 < NKT and not (qt > 0 and kt + 1 < NHS):
                if kt >= 1:
                    wait("tensor", "sC", exp_v[kt - 1])
                sc_T[kt + 1] = scores_mms(kt + 1)
            if not ctx_bank_waited:
                for pbi in (2, 3, 6):
                    semx, valx = bank_free[pbi]
                    wait("tensor", semx, valx)
                ctx_bank_waited = True
            wait("tensor", "sC", exp_v[kt])
            for c in range(NCH):
                pbi = 2 + c // 2
                half = c % 2
                if kt == 0 and half == 0 and pbi in (4, 5):
                    semx, valx = bank_free[pbi]
                    wait("tensor", semx, valx)
                lhsT = C_[:, kt * 1024 + c * 128: kt * 1024 + c * 128 + 128]
                st = (kt == 0 and half == 0)
                spf = (kt == NKT - 1 and half == 1)
                t_mm(pb[pbi][:, half * 256:(half + 1) * 256], lhsT, ebuf,
                     st, spf)
            kt_T[kt] = t_mm(pb[6][:, 0:256], ONE_T[:], ebuf,
                            kt == 0, kt == NKT - 1, inc=True)
        ctx_T = P_cnt["sT"]
        bank_free[0] = ("sC", P_cnt["sC"])
        bank_free[1] = ("sC", P_cnt["sC"])
        prev_kt_T = kt_T

        if qt + 2 < NQ:
            wait("sync", "sT", ctx_T)
            load_q(qt + 2)

        wait("vector", "sT", ctx_T)
        v_op("drain")
        v_op("reciprocal", FP[:, 0:256], pb[6][:, 0:256])
        v_op("drain")
        relvs = []
        for c in range(NCH):
            pbi = 2 + c // 2
            half = c % 2
            relvs.append(
                v_op("tensor_tensor", A[:, CX + c * 256: CX + (c + 1) * 256],
                     pb[pbi][:, half * 256:(half + 1) * 256], FP[:, 0:256],
                     ALU.mult, inc=True))
        relv = relvs[-1]
        v_op("drain")
        for i in range(4):
            bank_free[2 + i] = ("sV", relv)
        bank_free[6] = ("sV", relv)

        do_hoist = qt + 1 < NQ
        if do_hoist:
            nqcols = QB + ((qt + 1) % 2) * 2048
            cs = [0] * max(NHS, 2)
            ce = [0] * NHE

            def mk_ce(kk):
                wait("scalar", "sT", cs[kk])
                wait("scalar", "sT", kt_T[NKT - 4 + kk])
                return c_op("activation",
                            A[:, EB + (kk % 4) * 256: EB + (kk % 4 + 1) * 256],
                            pb[kk % 2][:, 0:256], AF.Exp, inc=True,
                            scale=1.0 / 32.0, bias=SM[:, 0:1])

            wait("tensor", "q%d" % ((qt + 1) % 2), qload[qt + 1])
            wait("tensor", "sC", exp_v[NKT - 2])
            cs[0] = scores_mms(0, nqcols)
            wait("tensor", "sC", exp_v[NKT - 1])
            cs[1] = scores_mms(1, nqcols)
            ce[0] = mk_ce(0)
            if NHS > 2:
                wait("tensor", "sC", ce[0])
                cs[2] = scores_mms(2, nqcols)
            if NHE > 1:
                ce[1] = mk_ce(1)

        def do_outproj(sub, bk0, bk1, defer_spill=False):
            tt = qt * 2 + sub
            slot = tt % 2
            xchan = "xr%d" % slot
            wait("tensor", xchan, xrl_t[slot])
            wait("tensor", "sV", boutb_v)
            if sub == 0:
                wait("tensor", "wo", wout_done)
            opvh = [0, 0]
            for half, pbi in ((0, bk0), (1, bk1)):
                sem, val = bank_free[pbi]
                wait("tensor", sem, val)
                for c in range(NCH):
                    if sub == 0 and half == 0:
                        wait("tensor", "sV", relvs[c])
                    lhsT = A[:, CX + c * 256 + sub * 128:
                             CX + c * 256 + sub * 128 + 128]
                    rhs = A[:, WO + c * 1024 + half * 512:
                            WO + c * 1024 + (half + 1) * 512]
                    t_mm(pb[pbi][:], lhsT, rhs, c == 0, False)
                t_mm(pb[pbi][:], IDT[:],
                     A[:, 15360 + slot * 1024 + half * 512: 15360 + slot * 1024 + (half + 1) * 512],
                     False, False)
                opvh[half] = t_mm(pb[pbi][:], ONE_T[0:1, :],
                                  BOUTB[0:1, half * 512:(half + 1) * 512],
                                  False, True, inc=True)
            opv = opvh[1]
            xres_free[slot] = opv
            wait("vector", "sT", opvh[0])
            v_op("drain")
            v_op("bn_stats", BN[:, 0:6], pb[bk0][:])
            wait("vector", "sT", opv)
            v_op("bn_stats", BN[:, 6:12], pb[bk1][:])
            v_op("drain")
            bnv = v_op("bn_aggr", BN[:, 12:14], BN[:, 0:12], inc=True)
            wait("scalar", "sV", bnv)
            c_op("activation", BN[:, 14:15], BN[:, 13:14], AF.Ln,
                 bias=SM[:, 1:2])
            c_op("drain")
            crv = c_op("activation", BN[:, 14:15], BN[:, 14:15], AF.Exp,
                       inc=True, scale=-0.5)
            wait("vector", "sC", crv)
            v_op("tensor_scalar", A[:, 15360 + slot * 1024: 15360 + slot * 1024 + 512],
                 pb[bk0][:], BN[:, 12:13], BN[:, 14:15],
                 ALU.subtract, ALU.mult)
            hv = v_op("tensor_scalar", A[:, 15360 + slot * 1024 + 512: 15360 + (slot + 1) * 1024],
                      pb[bk1][:], BN[:, 12:13], BN[:, 14:15],
                      ALU.subtract, ALU.mult, inc=True)
            bank_free[bk0] = ("sV", hv)
            bank_free[bk1] = ("sV", hv)
            if not defer_spill:
                wait("scalar", "sV", hv)
                h_spill[slot] = dma("scalar", "h%d" % slot,
                                    hsp_d[tt * 128:(tt + 1) * 128, :],
                                    A[:, 15360 + slot * 1024: 15360 + (slot + 1) * 1024])
            return hv, tt, slot

        do_outproj(0, 7, 6)
        if do_hoist:
            if NHS > 3:
                wait("tensor", "sC", ce[1])
                cs[3] = scores_mms(3, nqcols)
            if NHE > 2:
                ce[2] = mk_ce(2)
            if NHE > 3:
                ce[3] = mk_ce(3)
        hv1, tt1, slot1 = do_outproj(1, 4, 5, defer_spill=True)
        if do_hoist:
            if NHS > 4:
                wait("tensor", "sC", ce[2])
                cs[4] = scores_mms(4, nqcols)
            if NHS > 5:
                wait("tensor", "sC", ce[3])
                cs[5] = scores_mms(5, nqcols)
            carry_exp = tuple(ce)
            carry_sc = tuple(cs)
        wait("scalar", "sV", hv1)
        h_spill[slot1] = dma("scalar", "h%d" % slot1,
                             hsp_d[tt1 * 128:(tt1 + 1) * 128, :],
                             A[:, 15360 + slot1 * 1024: 15360 + (slot1 + 1) * 1024])

        while (ha_next + 1) * QC <= (qt + 1) * 256:
            qh = ha_next
            wait("sync", "h0", P_cnt["h0"])
            wait("sync", "h1", P_cnt["h1"])
            for c in range(3):
                dma("sync", "ha%d" % qh,
                    A[:, HFA + c * S + qh * QC: HFA + c * S + (qh + 1) * QC],
                    hsp_d[qh * QC:(qh + 1) * QC, c * 128:(c + 1) * 128],
                    transpose=True)
            ha_next += 1

    attn_T_done = P_cnt["sT"]
    if stop_phase <= 2:
        return finish_early()

    wait("gpsimd", "sT", attn_T_done)
    cbf_dst = C_[:, CBF:CBF + 512].rearrange("p (c e) -> p c e", c=NCH)
    cbf_src = ctrl_d.rearrange("(c p) e -> p c e", p=128)
    cbf_done = dma("gpsimd", "cb", cbf_dst, cbf_src)
    wait("sync", "sT", attn_T_done)
    tie_done = dma("sync", "tie", C_[0:64, TIE:TIE + S], tie_d)

    wait("sync", "h0", P_cnt["h0"])
    wait("sync", "h1", P_cnt["h1"])
    hfch = {3: "hf1", 4: "hf2", 5: "hf3", 6: "hfm", 7: "hf0"}
    hf_done = {}
    for c in range(3, NCH):
        hf_done[c] = dma("sync", hfch[c], B[:, c * S:(c + 1) * S],
                         hsp_d[:, c * 128:(c + 1) * 128], transpose=True)

    LGW = min(512, QC)
    TQ = NT // 4
    build_q = [0, 0, 0, 0]
    pq_free_T = [0, 0]

    def routing_chain(qc):
        base = (qc % 2) * QC
        q0 = qc * QC
        pqb = PQ if qc % 2 == 0 else PQ2
        bnk = 5 if qc % 2 == 0 else 4
        v_op("drain")
        lg3 = FP[0:64, base:base + QC].rearrange("p (g t) -> p g t", t=16)
        v_op("reduce_max", GMX[0:64, 0:QG], lg3, mybir.AxisListType.X)
        v_op("drain")
        wait("vector", "sT", pq_free_T[qc % 2])
        pq3 = C_[0:64, pqb:pqb + QC].rearrange("p (g t) -> p g t", t=16)
        gmb = GMX[0:64, 0:QG].unsqueeze(2).broadcast_to([64, QG, 16])
        pv = v_op("tensor_tensor", pq3, lg3, gmb, ALU.is_equal, inc=True)
        v_op("drain")
        wait("tensor", "sV", pv)
        sem5, val5 = bank_free[bnk]
        wait("tensor", sem5, val5)
        ptv = 0
        for i in range(TQ):
            outv = pb[bnk][:, i * 32:(i + 1) * 32].bitcast(BF16)
            P_cnt["sT"] += 1
            op("tensor", lambda e, o=outv, bb=bnk,
               inn=C_[0:64, pqb + i * 128: pqb + (i + 1) * 128]:
               e.transpose(o, inn, IDT[0:64, 0:64]).then_inc(sems["sT"], 1))
            ptv = P_cnt["sT"]
        pq_free_T[qc % 2] = ptv
        wait("vector", "sT", ptv)
        pe_ev = v_op("tensor_copy",
                     C_[:, PTM + qc * TQ * 64: PTM + (qc + 1) * TQ * 64],
                     pb[bnk][:, 0:TQ * 32].bitcast(BF16), inc=True)
        bank_free[bnk] = ("sV", pe_ev)
        v_op("drain")
        if qc == 0:
            wait("vector", "sm", sm_done)
        for t in range(qc * TQ, (qc + 1) * TQ):
            dst = C_[:, PB + t * 512: PB + (t + 1) * 512].rearrange(
                "p (e g) -> p e g", e=64)
            in0 = C_[:, PTM + t * 64: PTM + (t + 1) * 64].unsqueeze(2
                ).broadcast_to([128, 64, 8])
            in1 = GLM[:, 0:8].unsqueeze(1).broadcast_to([128, 64, 8])
            v_op("tensor_tensor", dst, in0, in1, ALU.mult)
        pq_v = v_op("drain", inc=True)
        build_q[qc] = pq_v

    for qc in range(4):
        q0 = qc * QC
        base = (qc % 2) * QC
        for nb in range(QC // LGW):
            pbi = 6 + nb % 2
            sem, val = bank_free[pbi]
            wait("tensor", sem, val)
            wait("tensor", "cb", cbf_done)
            tv = 0
            for k in range(NCH):
                lhsT = C_[:, CBF + k * 64: CBF + (k + 1) * 64]
                if k < 3:
                    if k == 0 and nb == 0:
                        wait("tensor", "ha%d" % qc, P_cnt["ha%d" % qc])
                    rhs = A[:, HFA + k * S + q0 + nb * LGW:
                            HFA + k * S + q0 + (nb + 1) * LGW]
                else:
                    if qc == 0 and nb == 0:
                        wait("tensor", hfch[k], hf_done[k])
                    rhs = B[:, k * S + q0 + nb * LGW: k * S + q0 + (nb + 1) * LGW]
                tv = t_mm(pb[pbi][0:64, 0:LGW], lhsT, rhs, k == 0, k == NCH - 1,
                          inc=(k == NCH - 1))
            wait("vector", "sT", tv)
            wait("vector", "tie", tie_done)
            v_op("drain")
            lg_v = v_op("tensor_tensor", FP[0:64, base + nb * LGW: base + (nb + 1) * LGW],
                        pb[pbi][0:64, 0:LGW],
                        C_[0:64, TIE + q0 + nb * LGW: TIE + q0 + (nb + 1) * LGW],
                        ALU.add, inc=True)
            bank_free[pbi] = ("sV", lg_v)
        if qc >= 1:
            routing_chain(qc - 1)
    routing_chain(3)

    logits_T_done = P_cnt["sT"]

    if stop_phase <= 3:
        return finish_early()

    ht_free_T = [0, 0, 0, 0]
    ht_ld = [0, 0, 0, 0]
    wait("sync", "h0", P_cnt["h0"])
    wait("sync", "h1", P_cnt["h1"])
    wait("scalar", "h0", P_cnt["h0"])
    wait("scalar", "h1", P_cnt["h1"])
    f1_free_T = [logits_T_done] * 4
    inner_free_T = [0, 0, 0, 0]
    istg_free = [0, 0, 0, 0]
    is_marks = []
    HT4 = PQ + QC

    def ht4(slot):
        return C_[:, HT4 + slot * 1024: HT4 + (slot + 1) * 1024]

    def ht_chq(slot):
        if slot % 2 == 0:
            return "sync", "hg%d" % slot
        return "scalar", "hs%d" % slot

    def load_ht(ec, t):
        slot = t % 4
        qn, ch = ht_chq(slot)
        wait(qn, "sT", ht_free_T[slot])
        ht_ld[slot] = dma(qn, ch,
                          ht4(slot), hsp_d[t * 128:(t + 1) * 128, :])

    for ec in range(4):
        gev_v = 0
        gev_c = 0
        for t in range(min(3, NT)):
            load_ht(ec, t)
        for t in range(NT):
            slot = t % 4
            if t + 3 < NT:
                load_ht(ec, t + 3)
            wait("tensor", ht_chq(slot)[1], ht_ld[slot])
            if ec == 0 and t % (NT // 4) == 0:
                wait("tensor", "sV", build_q[t // (NT // 4)])
            b0 = (t % 2) * 2
            tv = 0
            for c in range(NCH):
                pbi = b0 + c // 4
                qtr = c % 4
                if qtr == 0:
                    sem, val = bank_free[pbi]
                    wait("tensor", sem, val)
                lhsT = ht4(slot)[:, c * 128:(c + 1) * 128]
                rhs = C_[:, PB + t * 512 + ec * 128: PB + t * 512 + (ec + 1) * 128]
                tv = t_mm(pb[pbi][:, qtr * 128:(qtr + 1) * 128], lhsT, rhs,
                          qtr == 0, qtr == 3, inc=(c == NCH - 1))
            ht_free_T[slot] = tv
            selview = A[:, 0:NCH * SELC].rearrange("p (c e g) -> p c e g",
                                                   c=NCH, e=16)
            if ec == 0:
                wait("scalar", "sT", tv)
                c_op("activation", selview[:, 0:4, :, t * 8:(t + 1) * 8],
                     pb[b0][:], AF.Identity)
                gev_c = c_op("activation",
                             selview[:, 4:8, :, t * 8:(t + 1) * 8],
                             pb[b0 + 1][:], AF.Identity, inc=True)
                gev_v = P_cnt["sV"]
                bank_free[b0] = ("sC", gev_c)
                bank_free[b0 + 1] = ("sC", gev_c)
            else:
                wait("vector", "sT", tv)
                gev_v = v_op("tensor_copy",
                             selview[:, 0:4, :, t * 8:(t + 1) * 8],
                             pb[b0][:], inc=True)
                wait("scalar", "sT", tv)
                gev_c = c_op("activation",
                             selview[:, 4:8, :, t * 8:(t + 1) * 8],
                             pb[b0 + 1][:], AF.Identity, inc=True)
                bank_free[b0] = ("sV", gev_v)
                bank_free[b0 + 1] = ("sC", gev_c)
        sel_c = gev_c
        sel_vv = gev_v

        f1_ld = [0, 0, 0, 0]
        f2_ld = [0, 0, 0, 0]

        def load_fw(esl_t):
            es_t = ec * 16 + esl_t
            slot_t = esl_t % 4
            wait("scalar", "sT", max(f1_free_T[slot_t], inner_free_T[slot_t]))
            f1_dst = B[:, F1B + slot_t * 512: F1B + (slot_t + 1) * 512].rearrange(
                "p (c f) -> p c f", c=NCH)
            f1_src = f1b_d[:, es_t * FW:(es_t + 1) * FW].rearrange(
                "(c p) f -> p c f", p=128)
            f1_ld[slot_t] = dma("scalar", "f1%d" % slot_t, f1_dst, f1_src)
            f2_dst = B[0:64, F2B + slot_t * 1024: F2B + (slot_t + 1) * 1024]
            f2_src = f2b_d[es_t:es_t + 1, :].rearrange("a (f d) -> (a f) d", f=FW)
            f2_ld[slot_t] = dma("scalar", "f2%d" % slot_t, f2_dst, f2_src)

        if ec == 0:
            wait("scalar", "fc1", fc1_done)
            wait("scalar", "fc2", fc2_done)
        load_fw(0)
        load_fw(1)
        load_fw(2)

        def f1_mms(esl_t):
            slot_t = esl_t % 4
            wait("tensor", "f1%d" % slot_t, f1_ld[slot_t])
            if esl_t == 0:
                wait("tensor", "sC", sel_c)
                wait("tensor", "sV", sel_vv)
            pbi_t = 4 + esl_t % 2
            sem_t, val_t = bank_free[pbi_t]
            wait("tensor", sem_t, val_t)
            tvt = 0
            for k in range(NCH):
                lhsT = B[:, F1B + slot_t * 512 + k * 64:
                         F1B + slot_t * 512 + (k + 1) * 64]
                rhs = A[:, k * SELC + esl_t * G: k * SELC + esl_t * G + G]
                tvt = t_mm(pb[pbi_t][0:64, 0:G], lhsT, rhs, k == 0,
                           k == NCH - 1, inc=(k == NCH - 1))
            f1_free_T[slot_t] = tvt
            return tvt

        def relu_evac(esl_t, tvt):
            slot_t = esl_t % 4
            es_t = ec * 16 + esl_t
            wait("scalar", "sT", tvt)
            wait("scalar", "sT", inner_free_T[slot_t])
            cvt = c_op("activation",
                       B[0:64, INB + slot_t * 256: INB + slot_t * 256 + G],
                       pb[4 + esl_t % 2][0:64, 0:G], AF.Relu, inc=True,
                       bias=FBT[:, es_t:es_t + 1])
            bank_free[4 + esl_t % 2] = ("sC", cvt)
            return cvt

        tv1 = f1_mms(0)
        cv_cur = relu_evac(0, tv1)
        for esl in range(16):
            es = ec * 16 + esl
            slot = esl % 4
            if esl + 3 < 16:
                load_fw(esl + 3)
            cv = cv_cur
            if esl + 1 < 16:
                tv1 = f1_mms(esl + 1)
                cv_cur = relu_evac(esl + 1, tv1)
            wait("tensor", "f2%d" % slot, f2_ld[slot])
            wait("tensor", "sC", cv)
            for g0 in range(0, G, 128):
                gw = min(128, G - g0)
                gi = (esl * 2 + g0 // 128) % 3
                pb_pair = ((6, 7), (2, 3), (0, 1))[gi]
                for pbj in pb_pair:
                    sem, val = bank_free[pbj]
                    wait("tensor", sem, val)
                tv = 0
                for half in range(2):
                    pbj = pb_pair[half]
                    lhsT = B[0:64, INB + slot * 256 + g0:
                             INB + slot * 256 + g0 + gw]
                    rhs = B[0:64, F2B + slot * 1024 + half * 512:
                            F2B + slot * 1024 + (half + 1) * 512]
                    tv = t_mm(pb[pbj][0:gw, :], lhsT, rhs, True, True,
                              inc=(half == 1))
                istg = (esl * 2 + g0 // 128) % 4
                ichan = "is%d" % istg
                wait("vector", "sT", tv)
                wait("vector", ichan, istg_free[istg])
                v_op("drain")
                iv = v_op("tensor_copy",
                          B[0:gw, ISG + istg * 1024: ISG + istg * 1024 + 512],
                          pb[pb_pair[0]][0:gw, :], inc=True)
                wait("scalar", "sT", tv)
                wait("scalar", ichan, istg_free[istg])
                ic = c_op("activation",
                          B[0:gw, ISG + istg * 1024 + 512: ISG + (istg + 1) * 1024],
                          pb[pb_pair[1]][0:gw, :], AF.Identity, inc=True)
                bank_free[pb_pair[0]] = ("sV", iv)
                bank_free[pb_pair[1]] = ("sC", ic)
                wait("gpsimd", "sV", iv)
                wait("gpsimd", "sC", ic)
                istg_free[istg] = dma(
                    "gpsimd", ichan,
                    intersp_d[es:es + 1, g0:g0 + gw, :].rearrange("a g d -> (a g) d"),
                    B[0:gw, ISG + istg * 1024: ISG + (istg + 1) * 1024])
            inner_free_T[slot] = tv
        is_marks.append({isc: P_cnt[isc]
                         for isc in ("is0", "is1", "is2", "is3")})

    if stop_phase <= 4:
        return finish_early()

    p4_T_done = P_cnt["sT"]
    it_free_T = [0, 0, 0, 0, 0]
    pt_free_T = [0, 0]
    ht2_free = [0, 0, 0, 0]
    y_slot_dma = [0, 0]
    pt_ld = [0, 0]
    ig_ld = [0, 0, 0, 0, 0]
    ib_ld = [0, 0, 0, 0, 0]
    hld_t = [0, 0, 0, 0]

    wait("scalar", "sT", p4_T_done)

    def pe_pt(t):
        slot = t % 2
        pbk = pb[4 + slot]
        sem_p, val_p = bank_free[4 + slot]
        wait("tensor", sem_p, val_p)
        tvv = 0
        for c in range(4):
            outv = pbk[:, c * 64:(c + 1) * 64].bitcast(BF16)
            P_cnt["sT"] += 1
            op("tensor", lambda e, o=outv,
               inn=C_[:, PB + t * 512 + c * 128: PB + t * 512 + (c + 1) * 128]:
               e.transpose(o, inn, IDT[:]).then_inc(sems["sT"], 1))
            tvv = P_cnt["sT"]
        wait("vector", "sT", tvv)
        wait("vector", "sT", pt_free_T[slot])
        ev = v_op("tensor_copy",
                  B[:, PTO + slot * 512: PTO + (slot + 1) * 512],
                  pbk[:, 0:256].bitcast(BF16), inc=True)
        bank_free[4 + slot] = ("sV", ev)
        pt_ld[slot] = ev

    def load_it(t):
        slot = t % 5
        wait("sync", "sT", it_free_T[slot])
        wait("gpsimd", "sT", it_free_T[slot])
        for c in range(4):
            qn = "sync" if c < 2 else "gpsimd"
            for isc in ("is0", "is1", "is2", "is3"):
                wait(qn, isc, is_marks[c][isc])
            src_ap = intersp_d[16 * c:16 * (c + 1), t * 8:(t + 1) * 8, :]
            dst = B[:, ITO + slot * 4096 + c * 1024:
                    ITO + slot * 4096 + (c + 1) * 1024]
            if c < 2:
                ig_ld[slot] = dma("sync", "ig%d" % slot, dst, src_ap)
            else:
                ib_ld[slot] = dma("gpsimd", "ib%d" % slot, dst, src_ap)

    def load_h5(t):
        hslot = t % 4
        wait("scalar", "sT", ht2_free[hslot])
        hld_t[hslot] = dma("scalar", "hh%d" % hslot, ht4(hslot),
                           hsp_d[t * 128:(t + 1) * 128, :])

    pe_pt(0)
    if NT > 1:
        pe_pt(1)
    for tw in range(min(4, NT)):
        load_it(tw)
    for tw in range(min(3, NT)):
        load_h5(tw)

    for t in range(NT):
        slot = t % 5
        hslot = t % 4
        ptslot = t % 2
        wait("tensor", "sV", pt_ld[ptslot])
        wait("tensor", "ig%d" % slot, ig_ld[slot])
        wait("tensor", "ib%d" % slot, ib_ld[slot])
        wait("tensor", "hh%d" % hslot, hld_t[hslot])
        pr = ((0, 1), (2, 3), (6, 7))[t % 3]
        for half in range(2):
            sem, val = bank_free[pr[half]]
            wait("tensor", sem, val)
        tv = 0
        for half in range(2):
            for c in range(4):
                lhsT = B[:, PTO + ptslot * 512 + c * 128:
                         PTO + ptslot * 512 + (c + 1) * 128]
                rhs = B[:, ITO + slot * 4096 + c * 1024 + half * 512:
                        ITO + slot * 4096 + c * 1024 + (half + 1) * 512]
                t_mm(pb[pr[half]][:], lhsT, rhs, c == 0, False)
            tv = t_mm(pb[pr[half]][:], IDT[:],
                      ht4(hslot)[:, half * 512:(half + 1) * 512],
                      False, True, inc=True)
        it_free_T[slot] = tv
        ht2_free[hslot] = tv
        pt_free_T[ptslot] = tv
        if t + 2 < NT:
            pe_pt(t + 2)
        if t + 4 < NT:
            load_it(t + 4)
        if t + 3 < NT:
            load_h5(t + 3)
        wait("vector", "sT", tv)
        v_op("drain")
        v_op("bn_stats", BN[:, 0:6], pb[pr[0]][:])
        v_op("bn_stats", BN[:, 6:12], pb[pr[1]][:])
        v_op("drain")
        bnv = v_op("bn_aggr", BN[:, 12:14], BN[:, 0:12], inc=True)
        wait("scalar", "sV", bnv)
        c_op("activation", BN[:, 14:15], BN[:, 13:14], AF.Ln, bias=SM[:, 1:2])
        c_op("drain")
        crv = c_op("activation", BN[:, 14:15], BN[:, 14:15], AF.Exp,
                   inc=True, scale=-0.5)
        wait("vector", "sC", crv)
        yb = (t % 2) * 1024
        wait("vector", "y%d" % (t % 2), y_slot_dma[t % 2])
        v_op("tensor_scalar", FP[:, yb:yb + 512], pb[pr[0]][:],
             BN[:, 12:13], BN[:, 14:15], ALU.subtract, ALU.mult)
        yv = v_op("tensor_scalar", FP[:, yb + 512:yb + 1024], pb[pr[1]][:],
                  BN[:, 12:13], BN[:, 14:15], ALU.subtract, ALU.mult, inc=True)
        bank_free[pr[0]] = ("sV", yv)
        bank_free[pr[1]] = ("sV", yv)
        wait("scalar", "sV", yv)
        y_slot_dma[t % 2] = dma("scalar", "y%d" % (t % 2),
                                y_d[t * 128:(t + 1) * 128, :],
                                FP[:, yb:yb + 1024])

    for s in list(P_cnt.keys()):
        wait("gpsimd", s, P_cnt[s])
    return finalize()



def _prep_shared(W_qkv, b_qkv, W_out, b_out, controller, f1, f2, fbias, S):
    wqkv = np.ascontiguousarray(np.asarray(W_qkv, dtype=np.float32))
    bqkv = np.asarray(b_qkv, dtype=np.float32)
    bqkv_pc = np.stack([bqkv[m * 128:(m + 1) * 128] for m in range(16)], axis=1)
    bv_t = np.ascontiguousarray(np.tile(bqkv[2048:3072], (128, 1)))
    wout = np.ascontiguousarray(np.asarray(W_out, dtype=np.float32))
    bout_t = np.ascontiguousarray(np.tile(np.asarray(b_out, np.float32), (128, 1)))
    ctrl = np.ascontiguousarray(np.asarray(controller, np.float32).reshape(D, ES))
    f1r = np.ascontiguousarray(np.asarray(f1, np.float32).reshape(D, ES * FW))
    f2r = np.ascontiguousarray(np.asarray(f2, np.float32).reshape(ES, FW * D))
    fbias_t = np.ascontiguousarray(np.asarray(fbias, np.float32).reshape(ES, FW).T)
    tie_row = np.linspace(0.0, 1e-6, 16, dtype=np.float32)
    tie = np.ascontiguousarray(
        np.tile(np.tile(tie_row, S // 16), (ES, 1)).astype(ml_dtypes.bfloat16))
    glmask = np.ascontiguousarray(
        (np.arange(128)[:, None] // 16 == np.arange(8)[None, :]).astype(np.float32))
    ident = np.ascontiguousarray(np.eye(128, dtype=ml_dtypes.bfloat16))
    return {
        "wqkv": wqkv, "bqkv_pc": np.ascontiguousarray(bqkv_pc), "bv_t": bv_t,
        "wout": wout, "bout_t": bout_t, "ctrl": ctrl, "f1": f1r, "f2": f2r,
        "fbias_t": fbias_t, "tie": tie, "glmask": glmask, "ident": ident,
    }


def kernel(L0_attn_x, L0_attn_W_qkv, L0_attn_b_qkv, L0_attn_W_out, L0_attn_b_out,
           L0_ffn_controller, L0_ffn_f1, L0_ffn_f2, L0_ffn_bias):
    x = np.asarray(L0_attn_x, dtype=np.float32)
    Bb, S, _ = x.shape
    shared = _prep_shared(L0_attn_W_qkv, L0_attn_b_qkv, L0_attn_W_out,
                          L0_attn_b_out, L0_ffn_controller, L0_ffn_f1,
                          L0_ffn_f2, L0_ffn_bias, S)
    nc = build_nc(S=S)
    in_maps = []
    for b in range(Bb):
        m = dict(shared)
        m["x"] = np.ascontiguousarray(x[b])
        in_maps.append(m)
    res = run_bass_kernel_spmd(nc, in_maps, core_ids=list(range(Bb)))
    out = np.stack([np.asarray(r["y"], np.float32) for r in res.results], axis=0)
    return out


# version 31
# speedup vs baseline: 1.0215x; 1.0215x over previous
import numpy as np
import ml_dtypes
from contextlib import ExitStack

import concourse.bass as bass
import concourse.mybir as mybir
from concourse.bass_utils import run_bass_kernel_spmd

F32 = mybir.dt.float32
BF16 = mybir.dt.bfloat16
AF = mybir.ActivationFunctionType
ALU = mybir.AluOpType

D = 1024
NCH = 8
ES = 64
FW = 64

ENGINES = ("sync", "scalar", "vector", "tensor", "gpsimd")


def build_nc(S=4096, debug_scratch=False, stop_phase=99):
    nc = bass.Bass("TRN2", target_bir_lowering=False)
    G = S // 16
    NT = S // 128
    NQ = S // 256
    NB = S // 512
    NKT = S // 128
    QC = S // 4
    QG = QC // 16
    SELC = 16 * G

    SCR = "ExternalOutput" if debug_scratch else "Internal"

    x_d = nc.dram_tensor("x", [S, D], F32, kind="ExternalInput").ap()
    wqkv_d = nc.dram_tensor("wqkv", [D, 3 * D], F32, kind="ExternalInput").ap()
    bqkv_pc_d = nc.dram_tensor("bqkv_pc", [128, 16], F32, kind="ExternalInput").ap()
    bv_t_d = nc.dram_tensor("bv_t", [128, D], F32, kind="ExternalInput").ap()
    wout_d = nc.dram_tensor("wout", [D, D], F32, kind="ExternalInput").ap()
    bout_t_d = nc.dram_tensor("bout_t", [128, D], F32, kind="ExternalInput").ap()
    ctrl_d = nc.dram_tensor("ctrl", [D, ES], F32, kind="ExternalInput").ap()
    f1_d = nc.dram_tensor("f1", [D, ES * FW], F32, kind="ExternalInput").ap()
    f2_d = nc.dram_tensor("f2", [ES, FW * D], F32, kind="ExternalInput").ap()
    fbias_t_d = nc.dram_tensor("fbias_t", [FW, ES], F32, kind="ExternalInput").ap()
    tie_d = nc.dram_tensor("tie", [ES, S], BF16, kind="ExternalInput").ap()
    glmask_d = nc.dram_tensor("glmask", [128, 8], F32, kind="ExternalInput").ap()
    ident_d = nc.dram_tensor("ident", [128, 128], BF16, kind="ExternalInput").ap()

    y_d = nc.dram_tensor("y", [S, D], F32, kind="ExternalOutput").ap()

    xbf_d = nc.dram_tensor("xbf", [S, D], BF16, kind=SCR).ap()
    qsp_d = nc.dram_tensor("qsp", [NCH, 128, S], BF16, kind=SCR).ap()
    hsp_d = nc.dram_tensor("hsp", [S, D], BF16, kind=SCR).ap()
    intersp_d = nc.dram_tensor("intersp", [ES, G, D], BF16, kind=SCR).ap()
    f1b_d = nc.dram_tensor("f1b", [D, ES * FW], BF16, kind=SCR).ap()
    f2b_d = nc.dram_tensor("f2b", [ES, FW * D], BF16, kind=SCR).ap()

    ISG_END = 7168 + 4 * 1024 + 5 * 4096 + 2 * 512
    es_ = ExitStack()
    sb = lambda name, shape, dt: es_.enter_context(nc.sbuf_tensor(name, shape, dt))

    A = sb("arenaA", [128, 32768], BF16)
    B = sb("arenaB", [128, max(S * NCH, ISG_END)], BF16)
    C_ = sb("arenaC", [128, max(NT * 1024,
                                NT * 512 + S + 512 + NT * 64 + QC + 4096 + QC + 64)], BF16)
    FP = sb("arenaF", [128, 2048], F32)
    QS = sb("qstage", [128, 1024], BF16)
    BQ = sb("s_bqkv_pc", [128, 16], F32)
    FBT = sb("s_fbias_t", [64, 64], F32)
    SM = sb("smalls", [128, 4], F32)
    GLM = sb("s_glmask", [128, 8], F32)
    GMX = sb("s_gmax", [64, 64], F32)
    BN = sb("bnbuf", [128, 16], F32)
    ONE_T = sb("ones_t", [128, 128], BF16)
    IDT = sb("s_ident", [128, 128], BF16)
    BOUTB = sb("s_boutb", [1, 1024], BF16)

    pb = [es_.enter_context(nc.psum_tensor(f"pb{i}", [128, 512], F32))
          for i in range(8)]

    XB0 = 24576
    WO, QB, EB, CX = 0, 8192, 12288, 13312
    HFA = 17408
    PB = 0
    TIE = NT * 512
    CBF = TIE + S
    PTM = CBF + 512
    PQ = PTM + NT * 64
    PQ2 = PQ + QC + 4 * 1024
    F1B, INB, F2B, ISG = 0, 2048, 3072, 7168
    ITO = ISG + 4 * 1024
    PTO = ITO + 5 * 4096

    CHANNELS = ["xq0", "xq1", "xq2", "xq3", "sm", "wo", "bout", "cb", "tie", "hfm",
                "wc0", "wc1", "wc2", "wc3", "wc4", "wc5", "wc6", "wc7",
                "wck0", "wck1", "wck2", "wck3", "wck4", "wck5", "wck6", "wck7",
                "wcv0", "wcv1", "wcv2", "wcv3", "wcv4", "wcv5", "wcv6", "wcv7",
                "xa0", "xa1", "q0", "q1", "xr0", "xr1", "xr2", "xr3",
                "qs0", "qs1", "h0", "h1", "is0", "is1", "is2", "is3",
                "f10", "f11", "f12", "f13", "f20", "f21", "f22", "f23",
                "hf0", "hf1", "hf2", "hf3",
                "hg0", "hg1", "hg2", "hg3", "ig0", "ig1", "ig2",
                "ha0", "ha1", "ha2", "ha3", "y0", "y1",
                "fc1", "fc2", "ib0", "ib1", "ib2", "ib3", "ib4",
                "ig3", "ig4", "hh0", "hh1", "hh2", "hh3", "hs1", "hs3",
                "ic0", "ic1", "ic2", "ic3", "ic4"]
    P_cnt = {s: 0 for s in CHANNELS + ["sT", "sV", "sC"]}
    P_ops = {e: [] for e in ENGINES}
    sems = {}

    def op(eng, fn):
        P_ops[eng].append(fn)

    def dma(eng, chan, dst, src, **kw):
        P_cnt[chan] += 16
        op(eng, lambda e, d=dst, s=src, kw=kw, sm=chan:
           e.dma_start(d, s, **kw).then_inc(sems[sm], 16))
        return P_cnt[chan]

    waited = {}

    def wait(eng, sname, v):
        if v and v > waited.get((eng, sname), 0):
            waited[(eng, sname)] = v
            op(eng, lambda e, sm=sname, vv=v: e.wait_ge(sems[sm], vv))

    def t_mm(out, lhsT, rhs, start, stop, inc=False):
        if inc:
            P_cnt["sT"] += 1
            op("tensor", lambda e, o=out, l=lhsT, r=rhs, st=start, sp=stop:
               e.matmul(o, l, r, start=st, stop=sp).then_inc(sems["sT"], 1))
        else:
            op("tensor", lambda e, o=out, l=lhsT, r=rhs, st=start, sp=stop:
               e.matmul(o, l, r, start=st, stop=sp))
        return P_cnt["sT"]

    def eng_op(eng, sname, method, *args, inc=False, **kw):
        if inc:
            P_cnt[sname] += 1
            op(eng, lambda e, a=args, k=kw, m=method:
               getattr(e, m)(*a, **k).then_inc(sems[sname], 1))
        else:
            op(eng, lambda e, a=args, k=kw, m=method: getattr(e, m)(*a, **k))
        return P_cnt[sname]

    def v_op(method, *args, inc=False, **kw):
        return eng_op("vector", "sV", method, *args, inc=inc, **kw)

    def c_op(method, *args, inc=False, **kw):
        return eng_op("scalar", "sC", method, *args, inc=inc, **kw)

    def finalize():
        for s in list(P_cnt.keys()):
            sems[s] = es_.enter_context(nc.semaphore(s))
        block = es_.enter_context(nc.Block())
        with es_:
            for eng_name in ENGINES:
                ops = P_ops[eng_name]
                getattr(block, eng_name)(lambda e, ops=ops: [f(e) for f in ops])
        return nc

    def finish_early():
        for s in list(P_cnt.keys()):
            wait("gpsimd", s, P_cnt[s])
        return finalize()

    XCN = min(4, NB) if NB >= 1 else 1
    XCR = S // XCN
    dma("gpsimd", "xq0", xbf_d[0:XCR, :], x_d[0:XCR, :])
    for koff, pref in ((0, "wc"), (1024, "wck"), (2048, "wcv")):
        for k in range(NCH):
            dma("gpsimd", "%s%d" % (pref, k),
                A[:, k * 3072 + koff: k * 3072 + koff + 1024],
                wqkv_d[k * 128:(k + 1) * 128, koff:koff + 1024])
    dma("sync", "sm", BQ[:], bqkv_pc_d)
    dma("sync", "sm", GLM[:], glmask_d)
    dma("sync", "sm", IDT[:], ident_d)
    dma("sync", "sm", FBT[:], fbias_t_d)
    sm_done = dma("sync", "sm", FP[:, 0:1024], bv_t_d)
    v_op("memset", SM[:, 0:1], -4.0)
    v_op("memset", SM[:, 1:2], 1e-5)
    v_op("memset", ONE_T[:], 1.0)
    init_v = v_op("drain", inc=True)

    bank_free = [("sV", 0)] * 8
    xblk_T = [0, 0]

    xq_emitted = 1
    for b in range(NB):
        buf = b % 2
        xoff = XB0 + buf * 4096
        xchan = "xa%d" % buf
        while xq_emitted < XCN and xq_emitted * XCR <= (b + 2) * 512:
            i = xq_emitted
            wait("gpsimd", "sT", P_cnt["sT"])
            dma("gpsimd", "xq%d" % i, xbf_d[i * XCR:(i + 1) * XCR, :],
                x_d[i * XCR:(i + 1) * XCR, :])
            xq_emitted += 1
        wait("sync", "xq%d" % min((b * 512) // XCR, XCN - 1), 16)
        wait("sync", "sT", xblk_T[buf])
        xv = 0
        for c in range(NCH):
            xv = dma("sync", xchan, A[:, xoff + c * 512: xoff + (c + 1) * 512],
                     xbf_d[b * 512:(b + 1) * 512, c * 128:(c + 1) * 128],
                     transpose=True)

        wait("tensor", xchan, xv)

        for kind in ("Q", "K"):
            koff = 0 if kind == "Q" else 1024
            for m in range(NCH):
                pbi = m % 4
                sem, val = bank_free[pbi]
                wait("tensor", sem, val)
                tv = 0
                for k in range(NCH):
                    if b == 0 and m == 0:
                        wait("tensor",
                             ("wc%d" if kind == "Q" else "wck%d") % k, 16)
                    lhsT = A[:, k * 3072 + koff + m * 128:
                             k * 3072 + koff + m * 128 + 128]
                    rhs = A[:, xoff + k * 512: xoff + (k + 1) * 512]
                    tv = t_mm(pb[pbi][:], lhsT, rhs, k == 0, k == NCH - 1,
                              inc=(k == NCH - 1))
                wait("scalar", "sT", tv)
                wait("scalar", "sm", sm_done)
                if kind == "Q":
                    slot = m % 2
                    qchan = "qs%d" % slot
                    wait("scalar", qchan, P_cnt[qchan])
                    qsl = QS[:, slot * 512: slot * 512 + 512]
                    cv = c_op("activation", qsl, pb[pbi][:], AF.Identity,
                              inc=True, bias=BQ[:, m:m + 1])
                    wait("scalar", "sC", cv)
                    dma("scalar", qchan, qsp_d[m, :, b * 512:(b + 1) * 512], qsl)
                else:
                    cv = c_op("activation",
                              B[:, m * S + b * 512: m * S + (b + 1) * 512],
                              pb[pbi][:], AF.Identity, inc=True,
                              bias=BQ[:, 8 + m:8 + m + 1])
                bank_free[pbi] = ("sC", cv)

        wait("vector", "sV", init_v)
        wait("vector", "sm", sm_done)
        for sub in range(4):
            tt = b * 4 + sub
            for half in range(2):
                pbi = 4 + (sub * 2 + half) % 4
                sem, val = bank_free[pbi]
                wait("tensor", sem, val)
                tv = 0
                for k in range(NCH):
                    if b == 0 and sub == 0 and half == 0:
                        wait("tensor", "wcv%d" % k, 16)
                    lhsT = A[:, xoff + k * 512 + sub * 128:
                             xoff + k * 512 + sub * 128 + 128]
                    rhs = A[:, k * 3072 + 2048 + half * 512:
                            k * 3072 + 2048 + (half + 1) * 512]
                    tv = t_mm(pb[pbi][:], lhsT, rhs, k == 0, k == NCH - 1,
                              inc=(k == NCH - 1))
                wait("vector", "sT", tv)
                vv = v_op("tensor_tensor",
                          C_[:, tt * 1024 + half * 512: tt * 1024 + (half + 1) * 512],
                          pb[pbi][:], FP[:, half * 512:(half + 1) * 512],
                          ALU.add, inc=True)
                bank_free[pbi] = ("sV", vv)
        xblk_T[buf] = P_cnt["sT"]

    qkv_T_done = P_cnt["sT"]
    if stop_phase <= 1:
        return finish_early()

    wait("gpsimd", "sT", qkv_T_done)
    for c in range(NCH):
        dma("gpsimd", "wo", A[:, WO + c * 1024: WO + (c + 1) * 1024],
            wout_d[c * 128:(c + 1) * 128, :])
    wout_done = P_cnt["wo"]
    bout_done = dma("sync", "bout", FP[:, 1024:2048], bout_t_d)
    fc1_done = 0
    for i in range(4):
        fc1_done = dma("gpsimd", "fc1", f1b_d[i * 256:(i + 1) * 256, :],
                       f1_d[i * 256:(i + 1) * 256, :])
    fc2_done = 0
    for i in range(4):
        fc2_done = dma("gpsimd", "fc2", f2b_d[i * 16:(i + 1) * 16, :],
                       f2_d[i * 16:(i + 1) * 16, :])

    qload = [0] * (NQ + 2)

    def load_q(qt):
        bufq = qt % 2
        dst = A[:, QB + bufq * 2048: QB + (bufq + 1) * 2048].rearrange(
            "p (c q) -> p c q", c=NCH)
        src = qsp_d[:, :, qt * 256:(qt + 1) * 256].rearrange("c p q -> p c q")
        qload[qt] = dma("sync", "q%d" % bufq, dst, src)

    wait("sync", "qs0", P_cnt["qs0"])
    wait("sync", "qs1", P_cnt["qs1"])
    wait("sync", "sT", qkv_T_done)
    load_q(0)
    if NQ > 1:
        load_q(1)

    xres_free = [0, 0]
    h_spill = [0, 0]
    HFCH = {3: "hf1", 4: "hf2", 5: "hf3", 6: "hfm", 7: "hf0"}
    EHF = (NKT - 4) * 128 if NKT >= 8 else 0
    wait("vector", "bout", bout_done)
    boutb_v = v_op("tensor_copy", BOUTB[:], FP[0:1, 1024:2048], inc=True)
    v_op("drain")

    wait("tensor", "sC", P_cnt["sC"])
    wait("tensor", "sV", P_cnt["sV"])
    carry_sc = (0, 0)
    carry_exp = ()
    ha_next = 0
    prev_kt_T = [0] * NKT
    for qt in range(NQ):
        qbuf = qt % 2
        qcols = QB + qbuf * 2048
        if qt == 0:
            wait("tensor", "q%d" % qbuf, qload[qt])
        for pbi in (0, 1, 7):
            sem, val = bank_free[pbi]
            wait("tensor", sem, val)

        xrl_t = [0, 0]
        for sub in range(2):
            tt = qt * 2 + sub
            slot = tt % 2
            wait("sync", "sT", xres_free[slot])
            wait("sync", "h%d" % slot, h_spill[slot])
            xrl_t[slot] = dma("sync", "xr%d" % slot,
                              A[:, 15360 + slot * 1024: 15360 + (slot + 1) * 1024],
                              xbf_d[tt * 128:(tt + 1) * 128, :])

        kt_T = [0] * NKT
        exp_v = [0] * NKT

        def scores_mms(kt, qc_t=None):
            spp = kt % 2
            qc_t = qcols if qc_t is None else qc_t
            tv = 0
            for k in range(NCH):
                lhsT = B[:, k * S + kt * 128: k * S + kt * 128 + 128]
                rhs = A[:, qc_t + k * 256: qc_t + (k + 1) * 256]
                tv = t_mm(pb[spp][:, 0:256], lhsT, rhs, k == 0, k == NCH - 1,
                          inc=(k == NCH - 1))
            return tv

        sc_T = [0] * (NKT + 1)
        if qt == 0:
            sc_T[0] = scores_mms(0)
        else:
            for kk, vv in enumerate(carry_sc):
                sc_T[kk] = vv
        ctx_bank_waited = False
        NHE = min(4, NKT)
        NHS = min(6, NKT)
        for kt in range(NKT):
            sp = kt % 2
            esl4 = kt % 4
            ebuf = A[:, EB + esl4 * 256: EB + (esl4 + 1) * 256]
            if qt > 0 and kt < NHE:
                exp_v[kt] = carry_exp[kt]
            else:
                wait("scalar", "sT", sc_T[kt])
                if kt >= 4:
                    wait("scalar", "sT", kt_T[kt - 4])
                exp_v[kt] = c_op("activation", ebuf, pb[sp][:, 0:256], AF.Exp,
                                 inc=True, scale=1.0 / 32.0, bias=SM[:, 0:1])
            if kt + 1 < NKT and not (qt > 0 and kt + 1 < NHS):
                if kt >= 1:
                    wait("tensor", "sC", exp_v[kt - 1])
                sc_T[kt + 1] = scores_mms(kt + 1)
            if not ctx_bank_waited:
                for pbi in (2, 3, 6):
                    semx, valx = bank_free[pbi]
                    wait("tensor", semx, valx)
                ctx_bank_waited = True
            wait("tensor", "sC", exp_v[kt])
            for c in range(NCH):
                pbi = 2 + c // 2
                half = c % 2
                if kt == 0 and half == 0 and pbi in (4, 5):
                    semx, valx = bank_free[pbi]
                    wait("tensor", semx, valx)
                lhsT = C_[:, kt * 1024 + c * 128: kt * 1024 + c * 128 + 128]
                st = (kt == 0 and half == 0)
                spf = (kt == NKT - 1 and half == 1)
                t_mm(pb[pbi][:, half * 256:(half + 1) * 256], lhsT, ebuf,
                     st, spf)
            kt_T[kt] = t_mm(pb[6][:, 0:256], ONE_T[:], ebuf,
                            kt == 0, kt == NKT - 1, inc=True)
        ctx_T = P_cnt["sT"]
        bank_free[0] = ("sC", P_cnt["sC"])
        bank_free[1] = ("sC", P_cnt["sC"])
        prev_kt_T = kt_T

        if qt == NQ - 1 and EHF > 0:
            wait("sync", "sT", kt_T[NKT - 5])
            wait("sync", "h0", P_cnt["h0"])
            wait("sync", "h1", P_cnt["h1"])
            for c in range(3, NCH):
                dma("sync", HFCH[c], B[:, c * S: c * S + EHF],
                    hsp_d[0:EHF, c * 128:(c + 1) * 128], transpose=True)

        if qt + 2 < NQ:
            wait("sync", "sT", ctx_T)
            load_q(qt + 2)

        wait("vector", "sT", ctx_T)
        v_op("drain")
        v_op("reciprocal", FP[:, 0:256], pb[6][:, 0:256])
        v_op("drain")
        relvs = []
        for c in range(NCH):
            pbi = 2 + c // 2
            half = c % 2
            relvs.append(
                v_op("tensor_tensor", A[:, CX + c * 256: CX + (c + 1) * 256],
                     pb[pbi][:, half * 256:(half + 1) * 256], FP[:, 0:256],
                     ALU.mult, inc=True))
        relv = relvs[-1]
        v_op("drain")
        for i in range(4):
            bank_free[2 + i] = ("sV", relv)
        bank_free[6] = ("sV", relv)

        do_hoist = qt + 1 < NQ
        if do_hoist:
            nqcols = QB + ((qt + 1) % 2) * 2048
            cs = [0] * max(NHS, 2)
            ce = [0] * NHE

            def mk_ce(kk):
                wait("scalar", "sT", cs[kk])
                wait("scalar", "sT", kt_T[NKT - 4 + kk])
                return c_op("activation",
                            A[:, EB + (kk % 4) * 256: EB + (kk % 4 + 1) * 256],
                            pb[kk % 2][:, 0:256], AF.Exp, inc=True,
                            scale=1.0 / 32.0, bias=SM[:, 0:1])

            wait("tensor", "q%d" % ((qt + 1) % 2), qload[qt + 1])
            wait("tensor", "sC", exp_v[NKT - 2])
            cs[0] = scores_mms(0, nqcols)
            wait("tensor", "sC", exp_v[NKT - 1])
            cs[1] = scores_mms(1, nqcols)
            ce[0] = mk_ce(0)
            if NHS > 2:
                wait("tensor", "sC", ce[0])
                cs[2] = scores_mms(2, nqcols)
            if NHE > 1:
                ce[1] = mk_ce(1)

        def do_outproj(sub, bk0, bk1, defer_spill=False):
            tt = qt * 2 + sub
            slot = tt % 2
            xchan = "xr%d" % slot
            wait("tensor", xchan, xrl_t[slot])
            wait("tensor", "sV", boutb_v)
            if sub == 0:
                wait("tensor", "wo", wout_done)
            opvh = [0, 0]
            for half, pbi in ((0, bk0), (1, bk1)):
                sem, val = bank_free[pbi]
                wait("tensor", sem, val)
                for c in range(NCH):
                    if sub == 0 and half == 0:
                        wait("tensor", "sV", relvs[c])
                    lhsT = A[:, CX + c * 256 + sub * 128:
                             CX + c * 256 + sub * 128 + 128]
                    rhs = A[:, WO + c * 1024 + half * 512:
                            WO + c * 1024 + (half + 1) * 512]
                    t_mm(pb[pbi][:], lhsT, rhs, c == 0, False)
                t_mm(pb[pbi][:], IDT[:],
                     A[:, 15360 + slot * 1024 + half * 512: 15360 + slot * 1024 + (half + 1) * 512],
                     False, False)
                opvh[half] = t_mm(pb[pbi][:], ONE_T[0:1, :],
                                  BOUTB[0:1, half * 512:(half + 1) * 512],
                                  False, True, inc=True)
            opv = opvh[1]
            xres_free[slot] = opv
            wait("vector", "sT", opvh[0])
            v_op("drain")
            v_op("bn_stats", BN[:, 0:6], pb[bk0][:])
            wait("vector", "sT", opv)
            v_op("bn_stats", BN[:, 6:12], pb[bk1][:])
            v_op("drain")
            bnv = v_op("bn_aggr", BN[:, 12:14], BN[:, 0:12], inc=True)
            wait("scalar", "sV", bnv)
            c_op("activation", BN[:, 14:15], BN[:, 13:14], AF.Ln,
                 bias=SM[:, 1:2])
            c_op("drain")
            crv = c_op("activation", BN[:, 14:15], BN[:, 14:15], AF.Exp,
                       inc=True, scale=-0.5)
            wait("vector", "sC", crv)
            v_op("tensor_scalar", A[:, 15360 + slot * 1024: 15360 + slot * 1024 + 512],
                 pb[bk0][:], BN[:, 12:13], BN[:, 14:15],
                 ALU.subtract, ALU.mult)
            hv = v_op("tensor_scalar", A[:, 15360 + slot * 1024 + 512: 15360 + (slot + 1) * 1024],
                      pb[bk1][:], BN[:, 12:13], BN[:, 14:15],
                      ALU.subtract, ALU.mult, inc=True)
            bank_free[bk0] = ("sV", hv)
            bank_free[bk1] = ("sV", hv)
            if not defer_spill:
                wait("scalar", "sV", hv)
                h_spill[slot] = dma("scalar", "h%d" % slot,
                                    hsp_d[tt * 128:(tt + 1) * 128, :],
                                    A[:, 15360 + slot * 1024: 15360 + (slot + 1) * 1024])
            return hv, tt, slot

        do_outproj(0, 7, 6)
        if do_hoist:
            if NHS > 3:
                wait("tensor", "sC", ce[1])
                cs[3] = scores_mms(3, nqcols)
            if NHE > 2:
                ce[2] = mk_ce(2)
            if NHE > 3:
                ce[3] = mk_ce(3)
        hv1, tt1, slot1 = do_outproj(1, 4, 5, defer_spill=True)
        if do_hoist:
            if NHS > 4:
                wait("tensor", "sC", ce[2])
                cs[4] = scores_mms(4, nqcols)
            if NHS > 5:
                wait("tensor", "sC", ce[3])
                cs[5] = scores_mms(5, nqcols)
            carry_exp = tuple(ce)
            carry_sc = tuple(cs)
        wait("scalar", "sV", hv1)
        h_spill[slot1] = dma("scalar", "h%d" % slot1,
                             hsp_d[tt1 * 128:(tt1 + 1) * 128, :],
                             A[:, 15360 + slot1 * 1024: 15360 + (slot1 + 1) * 1024])

        while (ha_next + 1) * QC <= (qt + 1) * 256:
            qh = ha_next
            wait("sync", "h0", P_cnt["h0"])
            wait("sync", "h1", P_cnt["h1"])
            for c in range(3):
                dma("sync", "ha%d" % qh,
                    A[:, HFA + c * S + qh * QC: HFA + c * S + (qh + 1) * QC],
                    hsp_d[qh * QC:(qh + 1) * QC, c * 128:(c + 1) * 128],
                    transpose=True)
            ha_next += 1

    attn_T_done = P_cnt["sT"]
    if stop_phase <= 2:
        return finish_early()

    wait("gpsimd", "sT", attn_T_done)
    cbf_dst = C_[:, CBF:CBF + 512].rearrange("p (c e) -> p c e", c=NCH)
    cbf_src = ctrl_d.rearrange("(c p) e -> p c e", p=128)
    cbf_done = dma("gpsimd", "cb", cbf_dst, cbf_src)
    wait("sync", "sT", attn_T_done)
    tie_done = dma("sync", "tie", C_[0:64, TIE:TIE + S], tie_d)

    wait("sync", "h0", P_cnt["h0"])
    wait("sync", "h1", P_cnt["h1"])
    hfch = HFCH
    hf_done = {}
    for c in range(3, NCH):
        hf_done[c] = dma("sync", hfch[c], B[:, c * S + EHF:(c + 1) * S],
                         hsp_d[EHF:, c * 128:(c + 1) * 128], transpose=True)

    LGW = min(512, QC)
    TQ = NT // 4
    build_q = [0, 0, 0, 0]
    pq_free_T = [0, 0]

    def routing_chain(qc):
        base = (qc % 2) * QC
        q0 = qc * QC
        pqb = PQ if qc % 2 == 0 else PQ2
        bnk = 5 if qc % 2 == 0 else 4
        v_op("drain")
        lg3 = FP[0:64, base:base + QC].rearrange("p (g t) -> p g t", t=16)
        v_op("reduce_max", GMX[0:64, 0:QG], lg3, mybir.AxisListType.X)
        v_op("drain")
        wait("vector", "sT", pq_free_T[qc % 2])
        pq3 = C_[0:64, pqb:pqb + QC].rearrange("p (g t) -> p g t", t=16)
        gmb = GMX[0:64, 0:QG].unsqueeze(2).broadcast_to([64, QG, 16])
        pv = v_op("tensor_tensor", pq3, lg3, gmb, ALU.is_equal, inc=True)
        v_op("drain")
        wait("tensor", "sV", pv)
        sem5, val5 = bank_free[bnk]
        wait("tensor", sem5, val5)
        ptv = 0
        for i in range(TQ):
            outv = pb[bnk][:, i * 32:(i + 1) * 32].bitcast(BF16)
            P_cnt["sT"] += 1
            op("tensor", lambda e, o=outv, bb=bnk,
               inn=C_[0:64, pqb + i * 128: pqb + (i + 1) * 128]:
               e.transpose(o, inn, IDT[0:64, 0:64]).then_inc(sems["sT"], 1))
            ptv = P_cnt["sT"]
        pq_free_T[qc % 2] = ptv
        wait("vector", "sT", ptv)
        pe_ev = v_op("tensor_copy",
                     C_[:, PTM + qc * TQ * 64: PTM + (qc + 1) * TQ * 64],
                     pb[bnk][:, 0:TQ * 32].bitcast(BF16), inc=True)
        bank_free[bnk] = ("sV", pe_ev)
        v_op("drain")
        if qc == 0:
            wait("vector", "sm", sm_done)
        for t in range(qc * TQ, (qc + 1) * TQ):
            dst = C_[:, PB + t * 512: PB + (t + 1) * 512].rearrange(
                "p (e g) -> p e g", e=64)
            in0 = C_[:, PTM + t * 64: PTM + (t + 1) * 64].unsqueeze(2
                ).broadcast_to([128, 64, 8])
            in1 = GLM[:, 0:8].unsqueeze(1).broadcast_to([128, 64, 8])
            v_op("tensor_tensor", dst, in0, in1, ALU.mult)
        pq_v = v_op("drain", inc=True)
        build_q[qc] = pq_v

    for qc in range(4):
        q0 = qc * QC
        base = (qc % 2) * QC
        for nb in range(QC // LGW):
            pbi = 6 + nb % 2
            sem, val = bank_free[pbi]
            wait("tensor", sem, val)
            wait("tensor", "cb", cbf_done)
            tv = 0
            for k in range(NCH):
                lhsT = C_[:, CBF + k * 64: CBF + (k + 1) * 64]
                if k < 3:
                    if k == 0 and nb == 0:
                        wait("tensor", "ha%d" % qc, P_cnt["ha%d" % qc])
                    rhs = A[:, HFA + k * S + q0 + nb * LGW:
                            HFA + k * S + q0 + (nb + 1) * LGW]
                else:
                    if qc == 0 and nb == 0:
                        wait("tensor", hfch[k], hf_done[k])
                    rhs = B[:, k * S + q0 + nb * LGW: k * S + q0 + (nb + 1) * LGW]
                tv = t_mm(pb[pbi][0:64, 0:LGW], lhsT, rhs, k == 0, k == NCH - 1,
                          inc=(k == NCH - 1))
            wait("vector", "sT", tv)
            wait("vector", "tie", tie_done)
            v_op("drain")
            lg_v = v_op("tensor_tensor", FP[0:64, base + nb * LGW: base + (nb + 1) * LGW],
                        pb[pbi][0:64, 0:LGW],
                        C_[0:64, TIE + q0 + nb * LGW: TIE + q0 + (nb + 1) * LGW],
                        ALU.add, inc=True)
            bank_free[pbi] = ("sV", lg_v)
        if qc >= 1:
            routing_chain(qc - 1)
    routing_chain(3)

    logits_T_done = P_cnt["sT"]

    if stop_phase <= 3:
        return finish_early()

    ht_free_T = [0, 0, 0, 0]
    ht_ld = [0, 0, 0, 0]
    wait("sync", "h0", P_cnt["h0"])
    wait("sync", "h1", P_cnt["h1"])
    wait("scalar", "h0", P_cnt["h0"])
    wait("scalar", "h1", P_cnt["h1"])
    f1_free_T = [logits_T_done] * 4
    inner_free_T = [0, 0, 0, 0]
    istg_free = [0, 0, 0, 0]
    is_marks = []
    HT4 = PQ + QC

    def ht4(slot):
        return C_[:, HT4 + slot * 1024: HT4 + (slot + 1) * 1024]

    def ht_chq(slot):
        if slot % 2 == 0:
            return "sync", "hg%d" % slot
        return "scalar", "hs%d" % slot

    def load_ht(ec, t):
        slot = t % 4
        qn, ch = ht_chq(slot)
        wait(qn, "sT", ht_free_T[slot])
        ht_ld[slot] = dma(qn, ch,
                          ht4(slot), hsp_d[t * 128:(t + 1) * 128, :])

    for ec in range(4):
        gev_v = 0
        gev_c = 0
        for t in range(min(3, NT)):
            load_ht(ec, t)
        for t in range(NT):
            slot = t % 4
            if t + 3 < NT:
                load_ht(ec, t + 3)
            wait("tensor", ht_chq(slot)[1], ht_ld[slot])
            if ec == 0 and t % (NT // 4) == 0:
                wait("tensor", "sV", build_q[t // (NT // 4)])
            b0 = (t % 2) * 2
            tv = 0
            for c in range(NCH):
                pbi = b0 + c // 4
                qtr = c % 4
                if qtr == 0:
                    sem, val = bank_free[pbi]
                    wait("tensor", sem, val)
                lhsT = ht4(slot)[:, c * 128:(c + 1) * 128]
                rhs = C_[:, PB + t * 512 + ec * 128: PB + t * 512 + (ec + 1) * 128]
                tv = t_mm(pb[pbi][:, qtr * 128:(qtr + 1) * 128], lhsT, rhs,
                          qtr == 0, qtr == 3, inc=(c == NCH - 1))
            ht_free_T[slot] = tv
            selview = A[:, 0:NCH * SELC].rearrange("p (c e g) -> p c e g",
                                                   c=NCH, e=16)
            if ec == 0:
                wait("scalar", "sT", tv)
                c_op("activation", selview[:, 0:4, :, t * 8:(t + 1) * 8],
                     pb[b0][:], AF.Identity)
                gev_c = c_op("activation",
                             selview[:, 4:8, :, t * 8:(t + 1) * 8],
                             pb[b0 + 1][:], AF.Identity, inc=True)
                gev_v = P_cnt["sV"]
                bank_free[b0] = ("sC", gev_c)
                bank_free[b0 + 1] = ("sC", gev_c)
            else:
                wait("vector", "sT", tv)
                gev_v = v_op("tensor_copy",
                             selview[:, 0:4, :, t * 8:(t + 1) * 8],
                             pb[b0][:], inc=True)
                wait("scalar", "sT", tv)
                gev_c = c_op("activation",
                             selview[:, 4:8, :, t * 8:(t + 1) * 8],
                             pb[b0 + 1][:], AF.Identity, inc=True)
                bank_free[b0] = ("sV", gev_v)
                bank_free[b0 + 1] = ("sC", gev_c)
        sel_c = gev_c
        sel_vv = gev_v

        f1_ld = [0, 0, 0, 0]
        f2_ld = [0, 0, 0, 0]

        def load_fw(esl_t):
            es_t = ec * 16 + esl_t
            slot_t = esl_t % 4
            wait("scalar", "sT", max(f1_free_T[slot_t], inner_free_T[slot_t]))
            f1_dst = B[:, F1B + slot_t * 512: F1B + (slot_t + 1) * 512].rearrange(
                "p (c f) -> p c f", c=NCH)
            f1_src = f1b_d[:, es_t * FW:(es_t + 1) * FW].rearrange(
                "(c p) f -> p c f", p=128)
            f1_ld[slot_t] = dma("scalar", "f1%d" % slot_t, f1_dst, f1_src)
            f2_dst = B[0:64, F2B + slot_t * 1024: F2B + (slot_t + 1) * 1024]
            f2_src = f2b_d[es_t:es_t + 1, :].rearrange("a (f d) -> (a f) d", f=FW)
            f2_ld[slot_t] = dma("scalar", "f2%d" % slot_t, f2_dst, f2_src)

        if ec == 0:
            wait("scalar", "fc1", fc1_done)
            wait("scalar", "fc2", fc2_done)
        load_fw(0)
        load_fw(1)
        load_fw(2)

        def f1_mms(esl_t):
            slot_t = esl_t % 4
            wait("tensor", "f1%d" % slot_t, f1_ld[slot_t])
            if esl_t == 0:
                wait("tensor", "sC", sel_c)
                wait("tensor", "sV", sel_vv)
            pbi_t = 4 + esl_t % 2
            sem_t, val_t = bank_free[pbi_t]
            wait("tensor", sem_t, val_t)
            tvt = 0
            for k in range(NCH):
                lhsT = B[:, F1B + slot_t * 512 + k * 64:
                         F1B + slot_t * 512 + (k + 1) * 64]
                rhs = A[:, k * SELC + esl_t * G: k * SELC + esl_t * G + G]
                tvt = t_mm(pb[pbi_t][0:64, 0:G], lhsT, rhs, k == 0,
                           k == NCH - 1, inc=(k == NCH - 1))
            f1_free_T[slot_t] = tvt
            return tvt

        def relu_evac(esl_t, tvt):
            slot_t = esl_t % 4
            es_t = ec * 16 + esl_t
            wait("scalar", "sT", tvt)
            wait("scalar", "sT", inner_free_T[slot_t])
            cvt = c_op("activation",
                       B[0:64, INB + slot_t * 256: INB + slot_t * 256 + G],
                       pb[4 + esl_t % 2][0:64, 0:G], AF.Relu, inc=True,
                       bias=FBT[:, es_t:es_t + 1])
            bank_free[4 + esl_t % 2] = ("sC", cvt)
            return cvt

        tv1 = f1_mms(0)
        cv_cur = relu_evac(0, tv1)
        for esl in range(16):
            es = ec * 16 + esl
            slot = esl % 4
            if esl + 3 < 16:
                load_fw(esl + 3)
            cv = cv_cur
            if esl + 1 < 16:
                tv1 = f1_mms(esl + 1)
                cv_cur = relu_evac(esl + 1, tv1)
            wait("tensor", "f2%d" % slot, f2_ld[slot])
            wait("tensor", "sC", cv)
            for g0 in range(0, G, 128):
                gw = min(128, G - g0)
                gi = (esl * 2 + g0 // 128) % 3
                pb_pair = ((6, 7), (2, 3), (0, 1))[gi]
                for pbj in pb_pair:
                    sem, val = bank_free[pbj]
                    wait("tensor", sem, val)
                tv = 0
                for half in range(2):
                    pbj = pb_pair[half]
                    lhsT = B[0:64, INB + slot * 256 + g0:
                             INB + slot * 256 + g0 + gw]
                    rhs = B[0:64, F2B + slot * 1024 + half * 512:
                            F2B + slot * 1024 + (half + 1) * 512]
                    tv = t_mm(pb[pbj][0:gw, :], lhsT, rhs, True, True,
                              inc=(half == 1))
                istg = (esl * 2 + g0 // 128) % 4
                ichan = "is%d" % istg
                wait("vector", "sT", tv)
                wait("vector", ichan, istg_free[istg])
                v_op("drain")
                iv = v_op("tensor_copy",
                          B[0:gw, ISG + istg * 1024: ISG + istg * 1024 + 512],
                          pb[pb_pair[0]][0:gw, :], inc=True)
                wait("scalar", "sT", tv)
                wait("scalar", ichan, istg_free[istg])
                ic = c_op("activation",
                          B[0:gw, ISG + istg * 1024 + 512: ISG + (istg + 1) * 1024],
                          pb[pb_pair[1]][0:gw, :], AF.Identity, inc=True)
                bank_free[pb_pair[0]] = ("sV", iv)
                bank_free[pb_pair[1]] = ("sC", ic)
                wait("gpsimd", "sV", iv)
                wait("gpsimd", "sC", ic)
                istg_free[istg] = dma(
                    "gpsimd", ichan,
                    intersp_d[es:es + 1, g0:g0 + gw, :].rearrange("a g d -> (a g) d"),
                    B[0:gw, ISG + istg * 1024: ISG + (istg + 1) * 1024])
            inner_free_T[slot] = tv
        is_marks.append({isc: P_cnt[isc]
                         for isc in ("is0", "is1", "is2", "is3")})

    if stop_phase <= 4:
        return finish_early()

    p4_T_done = P_cnt["sT"]
    it_free_T = [0, 0, 0, 0, 0]
    pt_free_T = [0, 0]
    ht2_free = [0, 0, 0, 0]
    y_slot_dma = [0, 0]
    pt_ld = [0, 0]
    ig_ld = [0, 0, 0, 0, 0]
    ib_ld = [0, 0, 0, 0, 0]
    ic_ld = [0, 0, 0, 0, 0]
    hld_t = [0, 0, 0, 0]

    def pe_pt(t):
        slot = t % 2
        pbk = pb[4 + slot]
        sem_p, val_p = bank_free[4 + slot]
        wait("tensor", sem_p, val_p)
        tvv = 0
        for c in range(4):
            outv = pbk[:, c * 64:(c + 1) * 64].bitcast(BF16)
            P_cnt["sT"] += 1
            op("tensor", lambda e, o=outv,
               inn=C_[:, PB + t * 512 + c * 128: PB + t * 512 + (c + 1) * 128]:
               e.transpose(o, inn, IDT[:]).then_inc(sems["sT"], 1))
            tvv = P_cnt["sT"]
        wait("vector", "sT", tvv)
        wait("vector", "sT", pt_free_T[slot])
        ev = v_op("tensor_copy",
                  B[:, PTO + slot * 512: PTO + (slot + 1) * 512],
                  pbk[:, 0:256].bitcast(BF16), inc=True)
        bank_free[4 + slot] = ("sV", ev)
        pt_ld[slot] = ev

    def load_it(t):
        slot = t % 5
        wait("sync", "sT", it_free_T[slot])
        wait("gpsimd", "sT", it_free_T[slot])
        wait("scalar", "sT", it_free_T[slot])
        for c in range(4):
            qn = ("sync", "sync", "scalar", "gpsimd")[c]
            for isc in ("is0", "is1", "is2", "is3"):
                wait(qn, isc, is_marks[c][isc])
            src_ap = intersp_d[16 * c:16 * (c + 1), t * 8:(t + 1) * 8, :]
            dst = B[:, ITO + slot * 4096 + c * 1024:
                    ITO + slot * 4096 + (c + 1) * 1024]
            if c < 2:
                ig_ld[slot] = dma("sync", "ig%d" % slot, dst, src_ap)
            elif c == 2:
                ic_ld[slot] = dma("scalar", "ic%d" % slot, dst, src_ap)
            else:
                ib_ld[slot] = dma("gpsimd", "ib%d" % slot, dst, src_ap)

    def load_h5(t):
        hslot = t % 4
        wait("scalar", "sT", max(ht2_free[hslot], p4_T_done))
        hld_t[hslot] = dma("scalar", "hh%d" % hslot, ht4(hslot),
                           hsp_d[t * 128:(t + 1) * 128, :])

    pe_pt(0)
    if NT > 1:
        pe_pt(1)
    for tw in range(min(4, NT)):
        load_it(tw)
    for tw in range(min(3, NT)):
        load_h5(tw)

    for t in range(NT):
        slot = t % 5
        hslot = t % 4
        ptslot = t % 2
        wait("tensor", "sV", pt_ld[ptslot])
        wait("tensor", "ig%d" % slot, ig_ld[slot])
        wait("tensor", "ic%d" % slot, ic_ld[slot])
        wait("tensor", "ib%d" % slot, ib_ld[slot])
        wait("tensor", "hh%d" % hslot, hld_t[hslot])
        pr = ((0, 1), (2, 3), (6, 7))[t % 3]
        for half in range(2):
            sem, val = bank_free[pr[half]]
            wait("tensor", sem, val)
        tv = 0
        for half in range(2):
            for c in range(4):
                lhsT = B[:, PTO + ptslot * 512 + c * 128:
                         PTO + ptslot * 512 + (c + 1) * 128]
                rhs = B[:, ITO + slot * 4096 + c * 1024 + half * 512:
                        ITO + slot * 4096 + c * 1024 + (half + 1) * 512]
                t_mm(pb[pr[half]][:], lhsT, rhs, c == 0, False)
            tv = t_mm(pb[pr[half]][:], IDT[:],
                      ht4(hslot)[:, half * 512:(half + 1) * 512],
                      False, True, inc=True)
        it_free_T[slot] = tv
        ht2_free[hslot] = tv
        pt_free_T[ptslot] = tv
        if t + 2 < NT:
            pe_pt(t + 2)
        if t + 4 < NT:
            load_it(t + 4)
        if t + 3 < NT:
            load_h5(t + 3)
        wait("vector", "sT", tv)
        v_op("drain")
        v_op("bn_stats", BN[:, 0:6], pb[pr[0]][:])
        v_op("bn_stats", BN[:, 6:12], pb[pr[1]][:])
        v_op("drain")
        bnv = v_op("bn_aggr", BN[:, 12:14], BN[:, 0:12], inc=True)
        wait("scalar", "sV", bnv)
        c_op("activation", BN[:, 14:15], BN[:, 13:14], AF.Ln, bias=SM[:, 1:2])
        c_op("drain")
        crv = c_op("activation", BN[:, 14:15], BN[:, 14:15], AF.Exp,
                   inc=True, scale=-0.5)
        wait("vector", "sC", crv)
        yb = (t % 2) * 1024
        wait("vector", "y%d" % (t % 2), y_slot_dma[t % 2])
        v_op("tensor_scalar", FP[:, yb:yb + 512], pb[pr[0]][:],
             BN[:, 12:13], BN[:, 14:15], ALU.subtract, ALU.mult)
        yv = v_op("tensor_scalar", FP[:, yb + 512:yb + 1024], pb[pr[1]][:],
                  BN[:, 12:13], BN[:, 14:15], ALU.subtract, ALU.mult, inc=True)
        bank_free[pr[0]] = ("sV", yv)
        bank_free[pr[1]] = ("sV", yv)
        wait("scalar", "sV", yv)
        y_slot_dma[t % 2] = dma("scalar", "y%d" % (t % 2),
                                y_d[t * 128:(t + 1) * 128, :],
                                FP[:, yb:yb + 1024])

    for s in list(P_cnt.keys()):
        wait("gpsimd", s, P_cnt[s])
    return finalize()



def _prep_shared(W_qkv, b_qkv, W_out, b_out, controller, f1, f2, fbias, S):
    wqkv = np.ascontiguousarray(np.asarray(W_qkv, dtype=np.float32))
    bqkv = np.asarray(b_qkv, dtype=np.float32)
    bqkv_pc = np.stack([bqkv[m * 128:(m + 1) * 128] for m in range(16)], axis=1)
    bv_t = np.ascontiguousarray(np.tile(bqkv[2048:3072], (128, 1)))
    wout = np.ascontiguousarray(np.asarray(W_out, dtype=np.float32))
    bout_t = np.ascontiguousarray(np.tile(np.asarray(b_out, np.float32), (128, 1)))
    ctrl = np.ascontiguousarray(np.asarray(controller, np.float32).reshape(D, ES))
    f1r = np.ascontiguousarray(np.asarray(f1, np.float32).reshape(D, ES * FW))
    f2r = np.ascontiguousarray(np.asarray(f2, np.float32).reshape(ES, FW * D))
    fbias_t = np.ascontiguousarray(np.asarray(fbias, np.float32).reshape(ES, FW).T)
    tie_row = np.linspace(0.0, 1e-6, 16, dtype=np.float32)
    tie = np.ascontiguousarray(
        np.tile(np.tile(tie_row, S // 16), (ES, 1)).astype(ml_dtypes.bfloat16))
    glmask = np.ascontiguousarray(
        (np.arange(128)[:, None] // 16 == np.arange(8)[None, :]).astype(np.float32))
    ident = np.ascontiguousarray(np.eye(128, dtype=ml_dtypes.bfloat16))
    return {
        "wqkv": wqkv, "bqkv_pc": np.ascontiguousarray(bqkv_pc), "bv_t": bv_t,
        "wout": wout, "bout_t": bout_t, "ctrl": ctrl, "f1": f1r, "f2": f2r,
        "fbias_t": fbias_t, "tie": tie, "glmask": glmask, "ident": ident,
    }


def kernel(L0_attn_x, L0_attn_W_qkv, L0_attn_b_qkv, L0_attn_W_out, L0_attn_b_out,
           L0_ffn_controller, L0_ffn_f1, L0_ffn_f2, L0_ffn_bias):
    x = np.asarray(L0_attn_x, dtype=np.float32)
    Bb, S, _ = x.shape
    shared = _prep_shared(L0_attn_W_qkv, L0_attn_b_qkv, L0_attn_W_out,
                          L0_attn_b_out, L0_ffn_controller, L0_ffn_f1,
                          L0_ffn_f2, L0_ffn_bias, S)
    nc = build_nc(S=S)
    in_maps = []
    for b in range(Bb):
        m = dict(shared)
        m["x"] = np.ascontiguousarray(x[b])
        in_maps.append(m)
    res = run_bass_kernel_spmd(nc, in_maps, core_ids=list(range(Bb)))
    out = np.stack([np.asarray(r["y"], np.float32) for r in res.results], axis=0)
    return out
